# version 10 (final)
# baseline (speedup 1.0000x reference)
import numpy as np

import concourse.bass as bass
import concourse.mybir as mybir
from concourse.bass_utils import run_bass_kernel_spmd

F32 = mybir.dt.float32
F32R = mybir.dt.float32r
BF16 = mybir.dt.bfloat16
AF = mybir.ActivationFunctionType
ALU = mybir.AluOpType
AX = mybir.AxisListType

B, T, D, DI = 4, 4096, 1024, 2048
W = 512
NCH = T // W
KD = D // 128
NDT = DI // 128
NH = 8
NNH = 32
TOK = 2048
E, FF = 8, 2048
TC = 256
KF = FF // 128
MO = D // 128
THT = TOK // 2
NTCH = THT // TC
NTCT = TOK // TC
LN_EPS = 1e-5
RMS_EPS = 1e-6
TKEPS = 1e-9
LND = float(np.log(float(D)))
CONST_VALS = [LN_EPS, -0.5, 1.0 / D, RMS_EPS, -1.0, 1.0, -1.0 / D,
              LN_EPS * D, 0.5 * LND]
ROWOF = [0, 4, 1, 5, 2, 6, 3, 7]


def build_program():
    nc = bass.Bass("TRN2", target_bir_lowering=False)
    guards = []

    def dram(name, shape, dt, kind="Internal"):
        return nc.dram_tensor(name, shape, dt, kind=kind).ap()

    def sb(name, shape, dt):
        g = nc.sbuf_tensor(name, shape, dt)
        t = g.__enter__()
        guards.append(g)
        return t

    def ps_(name, shape):
        g = nc.psum_tensor(name, shape, F32)
        t = g.__enter__()
        guards.append(g)
        return t

    x_full = dram("x_full", [T, D], F32, "ExternalInput")
    x_mine = dram("x_mine", [TOK, D], F32, "ExternalInput")
    W_in_d = dram("W_in_d", [D, 2 * DI], F32, "ExternalInput")
    Wx_my = dram("Wx_my", [DI, 65], F32, "ExternalInput")
    A_arr = dram("A_arr", [128, NDT * NNH], F32, "ExternalInput")
    dtb_a = dram("dtb_a", [128, NDT], F32, "ExternalInput")
    cvw_a = dram("cvw_a", [128, NDT * 4], F32, "ExternalInput")
    cvb_a = dram("cvb_a", [128, NDT], F32, "ExternalInput")
    DD_a = dram("DD_a", [128, NDT], F32, "ExternalInput")
    W_out_d = dram("W_out_d", [DI, D], F32, "ExternalInput")
    nw_a = dram("nw_a", [128, KD], F32, "ExternalInput")
    wmr_a = dram("wmr_a", [128, KD * 12], F32, "ExternalInput")
    w1T_d = dram("w1T_d", [E, D, FF], F32, "ExternalInput")
    w2T_d = dram("w2T_d", [E, FF, D], F32, "ExternalInput")
    ident_d = dram("ident_d", [128, 128], BF16, "ExternalInput")
    onesc_d = dram("onesc_d", [128, 1], BF16, "ExternalInput")
    hsel_d = dram("hsel_d", [128, 2], F32, "ExternalInput")
    cstp_d = dram("cstp_d", [128, 12], F32, "ExternalInput")
    out_half = dram("out_half", [TOK, D], F32, "ExternalOutput")

    winb = dram("winb", [32, 128, KD * 128], BF16)
    woutb = dram("woutb", [MO, 128, KF * 128], BF16)
    w1b = dram("w1b", [E, D, FF], BF16)
    w2b = dram("w2b", [E, FF, D], BF16)
    bc_d = dram("bc_d", [65, W], BF16)
    zblk = dram("zblk", [NTCT, 128, KD * TC], F32)
    gblk = dram("gblk", [NTCT, 128, KD * TC], F32)
    rb_d = dram("rb_d", [1, TOK], F32)
    cbT_d = dram("cbT_d", [8, TOK], BF16)
    hpblk = dram("hpblk", [2, NTCH, 128, KF * TC], BF16)
    s_in = dram("s_in", [NCH, 128, KD * W], F32)
    s_out_d = dram("s_out_d", [NCH, 128, KD * W], F32)
    wmr32_d = dram("wmr32_d", [128, KD * 12], F32)

    ident = sb("ident", [128, 128], BF16)
    onesc = sb("onesc", [128, 1], BF16)
    hsel = sb("hsel", [128, 2], F32)
    cstp = sb("cstp", [128, 12], F32)
    for _ci, _cv in enumerate(CONST_VALS):
        nc.const_aps.register(F32, _cv, cstp[:, _ci:_ci + 1])
    A_sb = sb("A_sb", [128, NDT * NNH], F32)
    dtb = sb("dtb", [128, NDT], F32)
    cvw = sb("cvw", [128, NDT * 4], F32)
    cvb = sb("cvb", [128, NDT], F32)
    DD = sb("DD", [128, NDT], F32)
    nw = sb("nw", [128, KD], F32)
    wmr = sb("wmr", [128, KD * 12], BF16)
    wx = sb("wx", [128, NDT * 65], BF16)
    state = sb("state", [128, NNH * NDT], F32)
    st_s = sb("st_s", [128, 16], F32)

    xt = sb("xt", [128, D], F32)
    lnsq = sb("lnsq", [128, D], F32)
    xnt = sb("xnt", [128, D], BF16)
    xnd = sb("xnd", [128, KD * W], BF16)
    xst = sb("xst", [128, 2 * (W + 3)], BF16)
    xtail = sb("xtail", [128, NDT * 3], BF16)
    xcv = sb("xcv", [128, NDT * W], BF16)
    ygb = xcv
    szb = sb("szb", [128, NDT * W], BF16)
    dtk = sb("dtk", [128, NH * W], BF16)
    uxb = sb("uxb", [128, NH * W], BF16)
    yac = sb("yac", [128, NH * W], F32)
    bbg = sb("bbg", [128, 2 * 2 * W], BF16)
    cbg = sb("cbg", [128, 2 * 2 * W], BF16)
    dtrb = sb("dtrb", [128, W], BF16)
    ab = sb("ab", [128, 2 * W], F32)
    ubx = sb("ubx", [128, 2 * W], BF16)
    uby = sb("uby", [128, 2 * W], BF16)
    junk = sb("junk", [128, 4], F32)
    hbb = sb("hbb", [128, 2 * W], BF16)
    dtf = sb("dtf", [128, W], F32)
    bcs = sb("bcs", [128, W], BF16)
    wins = sb("wins", [128, 2 * KD * 128], BF16)
    wouts = sb("wouts", [128, KF * 128], BF16)
    sev = sb("sev", [128, W], F32)

    znh = sb("znh", [128, KD * THT], BF16)
    w2e = sb("w2e", [128, KF * 128 * MO], BF16)
    w1s = sb("w1s", [128, KD * 256], BF16)
    hpb = sb("hpb", [128, 2 * TC], BF16)
    zsr = sb("zsr", [128, KD * 128], F32R)
    hrd = sb("hrd", [128, KF * TC], BF16)
    hsil = sb("hsil", [128, 2 * TC], F32)
    zsl = sb("zsl", [128, KD * TC], F32)
    zslf = zsl[:]
    zsq = sb("zsq", [128, 2 * TC], BF16)
    cbc = sb("cbc", [128, TC], BF16)
    cmb = sb("cmb", [128, 8], F32)
    cmbb = sb("cmbb", [128, 8], BF16)
    rt = sb("rt", [128, 60], F32)
    rbb = sb("rbb", [1, TC], F32)
    wmr32 = sb("wmr32", [128, KD * 12], F32R)
    rbcol = sb("rbcol", [128, 2], F32)
    rbc = sb("rbc", [128, TC], F32)
    cts = sb("cts", [8, TOK], BF16)

    fs0 = yac[:]
    fs1 = xcv[:].bitcast(F32)
    fgd = szb[:].bitcast(F32)
    fsg = dtk[:]
    fot = lnsq[:, 0:D]
    zstage = szb[:].bitcast(F32)

    cvf = w2e[:, 0:4096].bitcast(F32)
    cvo = w2e[:, 4096:8192]

    ps_a = ps_("ps_a", [128, 512])
    ps_b = ps_("ps_b", [128, 512])
    ps_h0 = ps_("ps_h0", [128, 512])
    ps_h1 = ps_("ps_h1", [128, 512])
    ps_mx = ps_("ps_mx", [128, 2048])

    sems, cnt = {}, {}
    for nm in ["c", "xl", "cv", "wi", "wo", "w1", "w2", "zr", "zw", "bcd",
               "bbr", "cc", "fin", "out", "pe", "dv", "ac", "hp", "ss", "cb2"]:
        g = nc.semaphore(nm)
        sems[nm] = g.__enter__()
        guards.append(g)
        cnt[nm] = 0

    prog = {"sync": [], "tensor": [], "vector": [], "scalar": [], "gpsimd": []}

    def dma(eng, out, in_, sname):
        cnt[sname] += 16
        def f(e, _o=out, _i=in_, _s=sname):
            e.dma_start(out=_o, in_=_i).then_inc(sems[_s], 16)
        prog[eng].append(f)
        return cnt[sname]

    def wait(eng, sname, val=None):
        v = cnt[sname] if val is None else val
        if v > 0:
            prog[eng].append(lambda e, _v=v, _s=sname: e.wait_ge(sems[_s], _v))

    def op(eng, fn, sname=None, k=1):
        if sname is None:
            prog[eng].append(fn)
            return None
        cnt[sname] += k
        def f(e, _fn=fn, _s=sname, _k=k):
            _fn(e).then_inc(sems[_s], _k)
        prog[eng].append(f)
        return cnt[sname]

    psum_user = {"ps_a": ("dv", 0), "ps_b": ("dv", 0), "ps_h0": ("dv", 0),
                 "ps_h1": ("dv", 0), "ps_mx": ("dv", 0)}

    def vfill():
        op("vector", lambda v: v.tensor_tensor(
            junk[:, 0:1], cstp[:, 0:1], cstp[:, 1:2], ALU.add))

    for dst, src in [(ident[:], ident_d), (onesc[:], onesc_d), (hsel[:], hsel_d),
                     (cstp[:], cstp_d),
                     (A_sb[:], A_arr), (dtb[:], dtb_a), (cvw[:], cvw_a),
                     (cvb[:], cvb_a), (DD[:], DD_a), (nw[:], nw_a)]:
        dma("sync", dst, src, "c")
    wait("scalar", "c")
    op("scalar", lambda s: s.activation(A_sb[:], A_sb[:], AF.Exp), "ac")
    wait("vector", "ac")
    op("vector", lambda v: v.tensor_scalar_mul(A_sb[:], A_sb[:], -1.0), "dv")
    op("vector", lambda v: v.memzero(state[:]), "dv")
    op("vector", lambda v: v.tensor_scalar_mul(DD[:], DD[:], 0.5), "dv")
    op("vector", lambda v: v.memzero(xtail[:]), "dv")

    tl = dma("sync", xt[:, 0:KD * 12], wmr_a, "c")
    wait("vector", "c", tl)
    t_xtc = op("vector", lambda v: v.tensor_copy(wmr[:], xt[:, 0:KD * 12]), "dv")
    for k in range(KD):
        op("vector", lambda v, _k=k: v.tensor_scalar_mul(
            wmr[:, _k * 12:(_k + 1) * 12], wmr[:, _k * 12:(_k + 1) * 12],
            nw[:, _k:_k + 1]), "dv")
    for k in range(KD):
        t_xtc = op("vector", lambda v, _k=k: v.tensor_scalar_mul(
            xt[:, _k * 12:(_k + 1) * 12], xt[:, _k * 12:(_k + 1) * 12],
            nw[:, _k:_k + 1]), "dv")
    wait("sync", "dv", t_xtc)
    t_w32o = dma("sync", wmr32_d, xt[:, 0:KD * 12], "c")
    wait("gpsimd", "c", t_w32o)
    WMR32_CV = dma("gpsimd", wmr32[:], wmr32_d, "cv")
    for half in range(2):
        wait("sync", "c", t_w32o)
        wait("sync", "dv", t_xtc)
        tl = dma("sync", xt[:, 0:8 * 65].rearrange("p (k c) -> p k c", k=8),
                 Wx_my[half * 1024:(half + 1) * 1024, :].rearrange(
                     "(k p) c -> p k c", p=128), "c")
        wait("vector", "c", tl)
        t_xtc = op("vector", lambda v, _h=half: v.tensor_copy(
            wx[:, _h * 8 * 65:(_h + 1) * 8 * 65], xt[:, 0:8 * 65]), "dv")

    for m in range(32):
        t1 = dma("gpsimd", cvf[:, 0:KD * 128].rearrange("p (k c) -> p k c", k=KD),
                 W_in_d[:, m * 128:(m + 1) * 128].rearrange("(k p) c -> p k c", p=128),
                 "cv")
        wait("vector", "cv", t1)
        t2 = op("vector", lambda v: v.tensor_copy(cvo[:, 0:KD * 128], cvf[:, 0:KD * 128]), "dv")
        wait("gpsimd", "dv", t2)
        dma("gpsimd", winb[m], cvo[:, 0:KD * 128], "cv")
    for m in range(MO):
        t1 = dma("gpsimd", cvf[:, 0:KF * 128].rearrange("p (k c) -> p k c", k=KF),
                 W_out_d[:, m * 128:(m + 1) * 128].rearrange("(k p) c -> p k c", p=128),
                 "cv")
        wait("scalar", "cv", t1)
        t2 = op("scalar", lambda s: s.copy(cvo[:, 0:KF * 128], cvf[:, 0:KF * 128]), "ac")
        wait("gpsimd", "ac", t2)
        dma("gpsimd", woutb[m], cvo[:, 0:KF * 128], "cv")
    for e in range(E):
        for r in range(KD):
            t1 = dma("gpsimd", cvf[:, 0:2048], w1T_d[e][r * 128:(r + 1) * 128, :], "cv")
            wait("vector", "cv", t1)
            t2 = op("vector", lambda v, _r=r: v.tensor_scalar_mul(
                cvo[:, 0:2048], cvf[:, 0:2048], nw[:, _r:_r + 1]), "dv")
            wait("gpsimd", "dv", t2)
            dma("gpsimd", w1b[e][r * 128:(r + 1) * 128, :], cvo[:, 0:2048], "cv")
        for r in range(KF):
            t1 = dma("gpsimd", cvf[:, 0:1024], w2T_d[e][r * 128:(r + 1) * 128, :], "cv")
            wait("scalar", "cv", t1)
            t2 = op("scalar", lambda s: s.copy(cvo[:, 0:1024], cvf[:, 0:1024]), "ac")
            wait("gpsimd", "ac", t2)
            dma("gpsimd", w2b[e][r * 128:(r + 1) * 128, :], cvo[:, 0:1024], "cv")
    CV_ALL = cnt["cv"]

    xt_last = {"dv": t_xtc, "ac": 0}

    def load_x(src_ap, sname="xl"):
        wait("sync", "dv", xt_last["dv"])
        wait("sync", "ac", xt_last["ac"])
        return dma("sync", xt[:], src_ap, sname)

    LND = float(np.log(float(D)))

    def emit_ln(tl):
        wait("vector", "xl", tl)
        wait("scalar", "xl", tl)
        op("vector", lambda v: v.tensor_reduce(st_s[:, 0:1], xt[:], AX.X, ALU.add))
        t2 = op("scalar", lambda s: s.activation(lnsq[:], xt[:], AF.Square), "ac")
        xt_last["ac"] = t2
        wait("vector", "ac", t2)
        op("vector", lambda v: v.tensor_reduce(st_s[:, 1:2], lnsq[:], AX.X, ALU.add))
        op("vector", lambda v: v.tensor_tensor(st_s[:, 2:3], st_s[:, 0:1], st_s[:, 0:1], ALU.mult))
        op("vector", lambda v: v.tensor_scalar_mul(st_s[:, 0:1], st_s[:, 0:1], 1.0 / D))
        t3 = op("vector", lambda v: v.scalar_tensor_tensor(
            st_s[:, 1:2], st_s[:, 2:3], cstp[:, 6:7], st_s[:, 1:2], ALU.mult, ALU.add), "dv")
        wait("scalar", "dv", t3)
        op("scalar", lambda s: s.activation(st_s[:, 1:2], st_s[:, 1:2], AF.Ln, bias=LN_EPS * D))
        t4 = op("scalar", lambda s: s.activation(st_s[:, 1:2], st_s[:, 1:2], AF.Exp,
                                                 scale=-0.5, bias=0.5 * LND), "ac")
        wait("vector", "ac", t4)
        t5 = op("vector", lambda v: v.tensor_scalar(
            xnt[:], xt[:], st_s[:, 0:1], st_s[:, 1:2], ALU.subtract, ALU.mult), "dv")
        xt_last["dv"] = t5
        return t5

    def emit_xn_transpose(coff, t5, f32=False):
        wait("tensor", "dv", t5)
        tdv = 0
        for k in range(KD):
            pnm = "ps_a" if k % 2 == 0 else "ps_b"
            pst = ps_a if k % 2 == 0 else ps_b
            wait("tensor", *psum_user[pnm])
            if f32:
                tpe = op("tensor", lambda t, _k=k, _p=pst: t.transpose(
                    _p.bitcast(BF16)[:, 0:128], xnt[:, _k * 128:(_k + 1) * 128], ident[:]), "pe")
                wait("vector", "pe", tpe)
                tdv = op("vector", lambda v, _k=k, _p=pst, _c=coff: v.tensor_copy(
                    zstage[:, _k * W + _c:_k * W + _c + 128], _p.bitcast(BF16)[:, 0:128]), "dv")
            else:
                tpe = op("tensor", lambda t, _k=k, _p=pst: t.transpose(
                    _p.bitcast(BF16)[:, 0:128], xnt[:, _k * 128:(_k + 1) * 128], ident[:]), "pe")
                wait("vector", "pe", tpe)
                tdv = op("vector", lambda v, _k=k, _p=pst, _c=coff: v.tensor_copy(
                    xnd[:, _k * W + _c:_k * W + _c + 128], _p.bitcast(BF16)[:, 0:128]), "dv")
            psum_user[pnm] = ("dv", tdv)
        return tdv

    for cc4 in range(4):
        wait("vector", "zw")
        for tt in range(4):
            row = (cc4 * 4 + tt) * 128
            tl = load_x(x_mine[row:row + 128, :])
            t5 = emit_ln(tl)
            emit_xn_transpose(tt * 128, t5, f32=True)
        wait("gpsimd", "dv")
        for sub in range(2):
            tcg = cc4 * 2 + sub
            dma("gpsimd", zblk[tcg],
                zstage.rearrange("p (k c) -> p k c", k=KD)[:, :, sub * TC:(sub + 1) * TC],
                "zw")

    def ffn_znrouter(l, th):
        for tci in range(NTCH):
            tcg = th * NTCH + tci
            wait("sync", "zw")
            wait("sync", "dv")
            tz = dma("sync", zsl[:], zblk[tcg], "zr")
            wait("vector", "zr", tz)
            tpe = 0
            for k in range(KD):
                zp = k % 2
                if k >= 2:
                    wait("vector", "pe", tpe)
                tq = op("vector", lambda v, _k=k, _z=zp: v.tensor_tensor(
                    zsq[:, _z * TC:(_z + 1) * TC], zslf[:, _k * TC:(_k + 1) * TC],
                    zslf[:, _k * TC:(_k + 1) * TC], ALU.mult), "dv")
                wait("tensor", "dv", tq)
                if k == 0:
                    wait("tensor", *psum_user["ps_b"])
                tpe = op("tensor", lambda t, _k=k, _z=zp: t.matmul(
                    ps_b[0:1, 0:TC], onesc[:], zsq[:, _z * TC:(_z + 1) * TC],
                    start=(_k == 0), stop=(_k == KD - 1)), "pe")
            wait("scalar", "pe", tpe)
            wait("scalar", "zr")
            op("scalar", lambda s: s.activation(dtf[0:1, 0:TC], ps_b[0:1, 0:TC],
                                                AF.Ln, scale=1.0 / D, bias=RMS_EPS))
            ta = op("scalar", lambda s: s.activation(
                rbb[0:1, 0:TC], dtf[0:1, 0:TC], AF.Exp, scale=-0.5), "ac")
            psum_user["ps_b"] = ("ac", ta)
            wait("gpsimd", "ac", ta)
            tr = dma("gpsimd", rb_d[0:1, tcg * TC:(tcg + 1) * TC],
                     rbb[0:1, 0:TC], "zr")
            wait("gpsimd", "zr", tr)
            wait("gpsimd", "dv")
            tb = dma("gpsimd", rbc[:], rb_d[0:1, tcg * TC:(tcg + 1) * TC].broadcast_to([128, TC]), "bbr")
            tb = dma("gpsimd", rbcol[:, 0:1],
                     rb_d[0:1, tcg * TC:tcg * TC + 128].rearrange("o (t x) -> (o t) x", x=1), "bbr")
            tb = dma("gpsimd", rbcol[:, 1:2],
                     rb_d[0:1, tcg * TC + 128:(tcg + 1) * TC].rearrange("o (t x) -> (o t) x", x=1), "bbr")
            wait("vector", "bbr", tb)
            wait("vector", "pe")
            tz2 = 0
            for k in range(KD):
                tz2 = op("vector", lambda v, _k=k, _tci=tci: v.tensor_tensor(
                    znh[:, _k * THT + _tci * TC:_k * THT + (_tci + 1) * TC],
                    zslf[:, _k * TC:(_k + 1) * TC], rbc[:], ALU.mult), "dv")
            for tt in range(TC // 128):
                wait("gpsimd", "pe")
                tzr = dma("gpsimd", zsr[:].rearrange("p (k c) -> p k c", k=KD),
                          zblk[tcg].rearrange("p (k c) -> p k c", k=KD)[:, :, tt * 128:(tt + 1) * 128],
                          "zr")
                wait("tensor", "zr", tzr)
                wait("tensor", "cv", WMR32_CV)
                wait("tensor", *psum_user["ps_b"])
                for k in range(KD):
                    op("tensor", lambda t, _k=k: t.matmul(
                        ps_b[0:128, 16:28],
                        zsr[:, _k * 128:(_k + 1) * 128],
                        wmr32[:, _k * 12:(_k + 1) * 12],
                        start=(_k == 0), stop=(_k == KD - 1)), "pe")
                emit_router(l, th, tci, tt)

    def emit_router(l, th, tci, tt):
        wait("vector", "pe")
        tv0 = op("vector", lambda v: v.tensor_copy(rt[:, 0:12], ps_b[0:128, 16:28]), "dv")
        psum_user["ps_b"] = ("dv", tv0)
        op("vector", lambda v, _t=tt: v.tensor_scalar(
            rt[:, 0:12], rt[:, 0:12], rbcol[:, _t:_t + 1], None, ALU.mult), "dv")
        vfill()
        for g_ in range(4):
            op("vector", lambda v, _g=g_: v.tensor_tensor(
                rt[:, 28 + _g:29 + _g], rt[:, 4 + 2 * _g:5 + 2 * _g],
                rt[:, 5 + 2 * _g:6 + 2 * _g], ALU.subtract))
        op("vector", lambda v: v.tensor_reduce(rt[:, 12:13], rt[:, 0:4], AX.X, ALU.max))
        op("vector", lambda v: v.tensor_scalar_mul(rt[:, 32:36], rt[:, 28:32], -1.0))
        op("vector", lambda v: v.tensor_scalar_mul(rt[:, 12:13], rt[:, 12:13], -1.0))
        tm = op("vector", lambda v: v.tensor_tensor(rt[:, 32:36], rt[:, 28:32], rt[:, 32:36], ALU.max), "dv")
        wait("scalar", "dv", tm)
        op("scalar", lambda s: s.activation(rt[:, 14:18], rt[:, 0:4], AF.Exp, bias=rt[:, 12:13]))
        te = op("scalar", lambda s: s.activation(rt[:, 36:40], rt[:, 32:36], AF.Exp, scale=-1.0), "ac")
        wait("vector", "ac", te)
        op("vector", lambda v: v.tensor_reduce(rt[:, 13:14], rt[:, 14:18], AX.X, ALU.add))
        op("vector", lambda v: v.tensor_scalar_add(rt[:, 36:40], rt[:, 36:40], 1.0))
        op("vector", lambda v: v.reciprocal(rt[:, 13:14], rt[:, 13:14]))
        op("vector", lambda v: v.reciprocal(rt[:, 36:40], rt[:, 36:40]))
        op("vector", lambda v: v.tensor_scalar(rt[:, 14:18], rt[:, 14:18], rt[:, 13:14], None, ALU.mult))
        op("vector", lambda v: v.tensor_scalar_add(rt[:, 40:44], rt[:, 36:40], TKEPS))
        op("vector", lambda v: v.tensor_tensor(rt[:, 18:19], rt[:, 14:15], rt[:, 15:16], ALU.max))
        op("vector", lambda v: v.reciprocal(rt[:, 40:44], rt[:, 40:44]))
        op("vector", lambda v: v.tensor_tensor(rt[:, 19:20], rt[:, 14:15], rt[:, 15:16], ALU.min))
        op("vector", lambda v: v.tensor_tensor(rt[:, 36:40], rt[:, 36:40], rt[:, 40:44], ALU.mult))
        op("vector", lambda v: v.tensor_tensor(rt[:, 20:21], rt[:, 16:17], rt[:, 17:18], ALU.max))
        op("vector", lambda v: v.tensor_tensor(rt[:, 21:22], rt[:, 16:17], rt[:, 17:18], ALU.min))
        op("vector", lambda v: v.tensor_tensor(rt[:, 22:23], rt[:, 18:19], rt[:, 20:21], ALU.min))
        op("vector", lambda v: v.tensor_tensor(rt[:, 23:24], rt[:, 18:19], rt[:, 20:21], ALU.is_ge))
        op("vector", lambda v: v.tensor_tensor(rt[:, 24:25], rt[:, 19:20], rt[:, 21:22], ALU.subtract))
        op("vector", lambda v: v.scalar_tensor_tensor(
            rt[:, 24:25], rt[:, 24:25], rt[:, 23:24], rt[:, 21:22], ALU.mult, ALU.add))
        op("vector", lambda v: v.tensor_tensor(rt[:, 26:27], rt[:, 18:19], rt[:, 20:21], ALU.max))
        op("vector", lambda v: v.tensor_tensor(rt[:, 25:26], rt[:, 22:23], rt[:, 24:25], ALU.max))
        op("vector", lambda v: v.tensor_scalar_add(rt[:, 26:27], rt[:, 26:27], TKEPS))
        op("vector", lambda v: v.scalar_tensor_tensor(
            rt[:, 27:28], rt[:, 25:26], cstp[:, 5:6], rt[:, 26:27], ALU.mult, ALU.add))
        op("vector", lambda v: v.tensor_scalar(rt[:, 44:48], rt[:, 14:18], rt[:, 25:26], None, ALU.is_ge))
        op("vector", lambda v: v.reciprocal(rt[:, 27:28], rt[:, 27:28]))
        op("vector", lambda v: v.tensor_tensor(rt[:, 48:52], rt[:, 14:18], rt[:, 44:48], ALU.mult))
        op("vector", lambda v: v.tensor_scalar(rt[:, 40:44], rt[:, 36:40], rt[:, 27:28], None, ALU.mult))
        op("vector", lambda v: v.tensor_scalar(rt[:, 52:56], rt[:, 28:32], 0.0, None, ALU.is_ge))
        op("vector", lambda v: v.tensor_tensor(rt[:, 48:52], rt[:, 48:52], rt[:, 40:44], ALU.mult))
        op("vector", lambda v: v.tensor_scalar(rt[:, 56:60], rt[:, 28:32], 0.0, None, ALU.is_lt))
        op("vector", lambda v: v.tensor_tensor(cmb[:, 0:4], rt[:, 48:52], rt[:, 52:56], ALU.mult))
        op("vector", lambda v: v.tensor_tensor(cmb[:, 4:8], rt[:, 48:52], rt[:, 56:60], ALU.mult))
        vfill()
        tcm = op("vector", lambda v: v.tensor_copy(cmbb[:], cmb[:]), "dv")
        wait("tensor", "dv", tcm)
        wait("tensor", *psum_user["ps_a"])
        tpe = op("tensor", lambda t: t.transpose(ps_a.bitcast(BF16)[0:8, 0:128], cmbb[:], ident[:]), "pe")
        wait("vector", "pe", tpe)
        col = th * THT + tci * TC + tt * 128
        tv = op("vector", lambda v, _c=col: v.tensor_copy(cts[0:8, _c:_c + 128], ps_a.bitcast(BF16)[0:8, 0:128]), "dv")
        psum_user["ps_a"] = ("dv", tv)
        if tci == NTCH - 1 and tt == TC // 128 - 1:
            wait("gpsimd", "dv", tv)
            wait("gpsimd", "bbr")
            dma("gpsimd", cbT_d[0:8, th * THT:(th + 1) * THT],
                cts[0:8, th * THT:(th + 1) * THT], "cb2")

    HP_READS = [0, 0]

    def ffn_hphase(l, th, e):
        ep = e % 2
        wait("gpsimd", "cv", CV_ALL)
        wait("gpsimd", "pe")
        wait("gpsimd", "hp", HP_READS[ep])
        dma("gpsimd", w2e[:].rearrange("p (k c) -> p k c", k=KF),
            w2b[e].rearrange("(k p) c -> p k c", p=128), "w2")
        for mg in range(8):
            wait("gpsimd", "pe")
            tw1 = dma("gpsimd", w1s[:].rearrange("p (k c) -> p k c", k=KD),
                      w1b[e].rearrange("(k p) c -> p k c", p=128)[:, :, mg * 256:(mg + 1) * 256],
                      "w1")
            wait("tensor", "w1", tw1)
            for tci in range(NTCH):
                tcg = th * NTCH + tci
                wait("gpsimd", "cb2")
                wait("gpsimd", "dv")
                tcb = dma("gpsimd", cbc[:], cbT_d[ROWOF[e]:ROWOF[e] + 1, tcg * TC:(tcg + 1) * TC].broadcast_to([128, TC]), "bbr")
                thp = 0
                for m in range(2):
                    psh = ps_h0 if m % 2 == 0 else ps_h1
                    pnm = "ps_h0" if m % 2 == 0 else "ps_h1"
                    wait("tensor", *psum_user[pnm])
                    tpe = 0
                    for k in range(KD):
                        tpe = op("tensor", lambda t, _k=k, _m=m, _p=psh, _tci=tci: t.matmul(
                            _p[:, 0:TC],
                            w1s[:, _k * 256 + _m * 128:_k * 256 + _m * 128 + 128],
                            znh[:, _k * THT + _tci * TC:_k * THT + (_tci + 1) * TC],
                            start=(_k == 0), stop=(_k == KD - 1)), "pe")
                    wait("scalar", "pe", tpe)
                    wait("scalar", "dv")
                    tsl = op("scalar", lambda s, _m=m, _p=psh: s.activation(
                        hsil[:, (_m % 2) * TC:((_m % 2) + 1) * TC], _p[:, 0:TC], AF.Silu), "ac")
                    psum_user[pnm] = ("ac", tsl)
                    wait("vector", "ac", tsl)
                    wait("vector", "bbr", tcb)
                    thp = op("vector", lambda v, _m=m: v.tensor_tensor(
                        hpb[:, _m * TC:(_m + 1) * TC],
                        hsil[:, (_m % 2) * TC:((_m % 2) + 1) * TC],
                        cbc[:], ALU.mult), "dv")
                wait("gpsimd", "dv", thp)
                dma("gpsimd",
                    hpblk[ep][tci].rearrange("p (k c) -> p k c", k=KF)[:, mg * 2:(mg + 1) * 2, :],
                    hpb[:].rearrange("p (m c) -> p m c", m=2), "hp")

    def ffn_w2phase(l, th, e):
        ep = e % 2
        wait("tensor", "w2")
        for tci in range(NTCH):
            tcg = th * NTCH + tci
            wait("sync", "pe")
            wait("sync", "hp")
            thr = dma("sync", hrd[:], hpblk[ep][tci], "hp")
            HP_READS[ep] = thr
            wait("tensor", "hp", thr)
            wait("tensor", *psum_user["ps_mx"])
            tpe = 0
            for mo in range(MO):
                for k in range(KF):
                    tpe = op("tensor", lambda t, _k=k, _mo=mo: t.matmul(
                        ps_mx[:, _mo * TC:(_mo + 1) * TC],
                        w2e[:, _k * 1024 + _mo * 128:_k * 1024 + _mo * 128 + 128],
                        hrd[:, _k * TC:(_k + 1) * TC],
                        start=(_k == 0), stop=(_k == KF - 1)), "pe")
            wait("sync", "zw")
            wait("sync", "dv")
            tz = dma("sync", zslf, zblk[tcg], "zr")
            wait("vector", "zr", tz)
            wait("vector", "pe", tpe)
            tv = 0
            for mo in range(MO):
                tv = op("vector", lambda v, _mo=mo: v.tensor_tensor(
                    zslf[:, _mo * TC:(_mo + 1) * TC], zslf[:, _mo * TC:(_mo + 1) * TC],
                    ps_mx[:, _mo * TC:(_mo + 1) * TC], ALU.add), "dv")
            psum_user["ps_mx"] = ("dv", tv)
            wait("sync", "dv", tv)
            dma("sync", zblk[tcg], zslf, "zw")

    def ffn_gfinal():
        for tcg in range(NTCT):
            wait("sync", "zw")
            wait("sync", "dv")
            wait("sync", "ac")
            tz = dma("sync", zslf, zblk[tcg], "zr")
            wait("scalar", "zr", tz)
            tg = op("scalar", lambda s: s.activation(zslf, zslf, AF.Silu), "ac")
            wait("gpsimd", "ac", tg)
            dma("gpsimd", gblk[tcg], zslf, "zw")

    ffn_units = []
    for l in range(2):
        for th in range(2):
            ffn_units.append(lambda _l=l, _th=th: ffn_znrouter(_l, _th))
            for e in range(E):
                ffn_units.append(lambda _l=l, _th=th, _e=e: ffn_hphase(_l, _th, _e))
                ffn_units.append(lambda _l=l, _th=th, _e=e: ffn_w2phase(_l, _th, _e))
    ffn_units.append(lambda: ffn_gfinal())
    ffn_pos = [0]

    def emit_ffn(n):
        while n > 0 and ffn_pos[0] < len(ffn_units):
            ffn_units[ffn_pos[0]]()
            ffn_pos[0] += 1
            n -= 1

    UPC = (len(ffn_units) + NCH - 1) // NCH
    NG = 2
    wins_pe = [0, 0]
    ss_prev = [0]

    for c in range(NCH):
        c0 = c * W
        if c == 0:
            wait("vector", "zw")
        for tt in range(4):
            row = c0 + tt * 128
            tl = load_x(x_full[row:row + 128, :])
            t5 = emit_ln(tl)
            emit_xn_transpose(tt * 128, t5)
        XND_DV = cnt["dv"]

        emit_ffn(UPC)

        for m in range(32):
            par = m % 2
            wait("sync", "pe", wins_pe[par])
            tw = dma("sync", wins[:, par * KD * 128:(par + 1) * KD * 128], winb[m], "wi")
            wait("tensor", "wi", tw)
            if m < 2:
                wait("tensor", "dv", XND_DV)
            pnm = "ps_a" if par == 0 else "ps_b"
            psx = ps_a if par == 0 else ps_b
            wait("tensor", *psum_user[pnm])
            tpe = 0
            for k in range(KD):
                tpe = op("tensor", lambda t, _k=k, _par=par, _p=psx: t.matmul(
                    _p[:], wins[:, _par * KD * 128 + _k * 128:_par * KD * 128 + (_k + 1) * 128],
                    xnd[:, _k * W:(_k + 1) * W],
                    start=(_k == 0), stop=(_k == KD - 1)), "pe")
            wins_pe[par] = tpe
            wait("vector", "pe", tpe)
            if m < 16:
                dt_i = m
                xb = (m % 2) * (W + 3)
                op("vector", lambda v, _b=xb, _d=dt_i: v.tensor_copy(
                    xst[:, _b:_b + 3], xtail[:, _d * 3:_d * 3 + 3]))
                op("vector", lambda v, _b=xb, _p=psx: v.tensor_copy(
                    xst[:, _b + 3:_b + 3 + W], _p[:]))
                xc = xcv[:, dt_i * W:(dt_i + 1) * W]
                op("vector", lambda v, _xc=xc, _b=xb, _d=dt_i: v.tensor_scalar(
                    _xc, xst[:, _b:_b + W], cvw[:, _d * 4:_d * 4 + 1], None, ALU.mult))
                op("vector", lambda v, _b=xb, _d=dt_i: v.tensor_copy(
                    xtail[:, _d * 3:_d * 3 + 3], xst[:, _b + W:_b + W + 3]))
                tev = 0
                for tap in range(1, 4):
                    tev = op("vector", lambda v, _xc=xc, _b=xb, _d=dt_i, _t=tap: v.scalar_tensor_tensor(
                        _xc, xst[:, _b + _t:_b + _t + W], cvw[:, _d * 4 + _t:_d * 4 + _t + 1],
                        _xc, ALU.mult, ALU.add), "dv")
            else:
                dt_i = m - 16
                tev = op("vector", lambda v, _d=dt_i, _p=psx: v.tensor_copy(
                    szb[:, _d * W:(_d + 1) * W], _p[:]), "dv")
            psum_user[pnm] = ("dv", tev)

        wait("scalar", "dv", cnt["dv"])
        for dt_i in range(NDT):
            op("scalar", lambda s, _d=dt_i: s.activation(
                xcv[:, _d * W:(_d + 1) * W], xcv[:, _d * W:(_d + 1) * W], AF.Silu,
                bias=cvb[:, _d:_d + 1]))
        tsz = 0
        for dt_i in range(NDT):
            tsz = op("scalar", lambda s, _d=dt_i: s.activation(
                szb[:, _d * W:(_d + 1) * W], szb[:, _d * W:(_d + 1) * W], AF.Silu), "ac")

        wait("tensor", "ac", tsz)
        wait("tensor", *psum_user["ps_a"])
        tpe = 0
        for k in range(NDT):
            tpe = op("tensor", lambda t, _k=k: t.matmul(
                ps_a[0:65, :], wx[:, _k * 65:(_k + 1) * 65], xcv[:, _k * W:(_k + 1) * W],
                start=(_k == 0), stop=(_k == NDT - 1)), "pe")
        wait("vector", "pe", tpe)
        wait("vector", "bcd")
        tbc = op("vector", lambda v: v.tensor_copy(bcs[0:65, :], ps_a[0:65, :]), "dv")
        psum_user["ps_a"] = ("dv", tbc)
        wait("gpsimd", "dv", tbc)
        wait("gpsimd", "bbr")
        tbd = dma("gpsimd", bc_d, bcs[0:65, :], "bcd")
        wait("gpsimd", "bcd", tbd)
        wait("gpsimd", "ac")
        tdr = dma("gpsimd", dtrb[:], bc_d[64:65, :].broadcast_to([128, W]), "bbr")

        for hf in range(2):
            wait("scalar", "bbr", tdr)
            wait("scalar", "dv")
            tdt = 0
            for i in range(NH):
                dt_g = hf * NH + i
                op("scalar", lambda s, _d=dt_g: s.activation(
                    dtf[:], dtrb[:], AF.Exp, bias=dtb[:, _d:_d + 1]))
                tdt = op("scalar", lambda s, _i=i: s.activation(
                    dtk[:, _i * W:(_i + 1) * W], dtf[:], AF.Ln, bias=1.0), "ac")

            grp_ready = {}
            dv_group_end = {}

            def fetch_group(gi, wait_dv):
                gp = gi % 2
                if wait_dv is not None:
                    wait("gpsimd", "dv", wait_dv)
                dma("gpsimd", bbg[:, gp * NG * W:(gp + 1) * NG * W],
                    bc_d[gi * NG:(gi + 1) * NG, :].unsqueeze(0).broadcast_to([128, NG, W]), "bbr")
                tb2 = dma("gpsimd", cbg[:, gp * NG * W:(gp + 1) * NG * W],
                          bc_d[32 + gi * NG:32 + (gi + 1) * NG, :].unsqueeze(0).broadcast_to([128, NG, W]), "bbr")
                grp_ready[gi] = tb2

            fetch_group(0, cnt["dv"])
            fetch_group(1, None)

            wait("vector", "ac", tdt)
            for i in range(NH):
                dt_g = hf * NH + i
                op("vector", lambda v, _i=i, _d=dt_g: v.tensor_tensor(
                    uxb[:, _i * W:(_i + 1) * W], dtk[:, _i * W:(_i + 1) * W],
                    xcv[:, _d * W:(_d + 1) * W], ALU.mult))
                op("vector", lambda v, _i=i, _d=dt_g: v.tensor_scalar(
                    yac[:, _i * W:(_i + 1) * W], xcv[:, _d * W:(_d + 1) * W],
                    DD[:, _d:_d + 1], None, ALU.mult))

            J = NNH * NH

            def emit_U(j):
                n_, i_ = j // NH, j % NH
                gi_ = n_ // NG
                goff_ = ((gi_ % 2) * NG + (n_ % NG)) * W
                if i_ == 0:
                    wait("vector", "bbr", grp_ready[gi_])
                op("vector", lambda v, _i=i_, _g=goff_, _p=j % 2: v.tensor_tensor(
                    ubx[:, _p * W:(_p + 1) * W], uxb[:, _i * W:(_i + 1) * W],
                    bbg[:, _g:_g + W], ALU.mult))

            emit_U(0)
            vfill()
            vfill()
            a_cons = {}
            for j in range(J):
                n, i = j // NH, j % NH
                gi = n // NG
                gp = gi % 2
                goff = (gp * NG + (n % NG)) * W
                apar = j % 2
                dt_g = hf * NH + i
                scol = n * NDT + dt_g
                if i == 0 and n % NG == 0 and 1 <= gi and gi + 1 < NNH // NG:
                    fetch_group(gi + 1, dv_group_end[gi - 1])
                if j >= 2:
                    wait("scalar", "dv", a_cons[j - 2])
                ta = op("scalar", lambda s, _i=i, _d=dt_g, _n=n, _ap=apar: s.activation(
                    ab[:, _ap * W:(_ap + 1) * W], dtk[:, _i * W:(_i + 1) * W], AF.Exp,
                    scale=A_sb[:, _d * NNH + _n:_d * NNH + _n + 1]), "ac")
                wait("vector", "ac", ta)
                tsc = op("vector", lambda v, _ap=apar, _sc=scol: v.tensor_tensor_scan(
                    hbb[:, _ap * W:(_ap + 1) * W], ab[:, _ap * W:(_ap + 1) * W],
                    ubx[:, _ap * W:(_ap + 1) * W],
                    state[:, _sc:_sc + 1], ALU.mult, ALU.add), "dv")
                a_cons[j] = tsc
                if j + 1 < J:
                    emit_U(j + 1)
                else:
                    vfill()
                op("vector", lambda v, _ap=apar, _g=goff: v.tensor_tensor(
                    uby[:, _ap * W:(_ap + 1) * W], hbb[:, _ap * W:(_ap + 1) * W],
                    cbg[:, _g:_g + W], ALU.mult))
                op("vector", lambda v, _ap=apar, _sc=scol: v.tensor_copy(
                    state[:, _sc:_sc + 1], hbb[:, _ap * W + W - 1:_ap * W + W]))
                op("vector", lambda v, _i=i, _ap=apar: v.tensor_tensor(
                    yac[:, _i * W:(_i + 1) * W], yac[:, _i * W:(_i + 1) * W],
                    uby[:, _ap * W:(_ap + 1) * W], ALU.add), "dv")
                if i == NH - 1:
                    dv_group_end[gi] = cnt["dv"]
            for i in range(NH):
                dt_g = hf * NH + i
                op("vector", lambda v, _i=i, _d=dt_g: v.tensor_tensor(
                    ygb[:, _d * W:(_d + 1) * W], yac[:, _i * W:(_i + 1) * W],
                    szb[:, _d * W:(_d + 1) * W], ALU.mult), "dv")

        YG_DV = cnt["dv"]
        for m in range(MO):
            wait("sync", "pe")
            tw = dma("sync", wouts[:], woutb[m], "wo")
            wait("tensor", "wo", tw)
            if m == 0:
                wait("tensor", "dv", YG_DV)
            pnm = "ps_a" if m % 2 == 0 else "ps_b"
            psx = ps_a if m % 2 == 0 else ps_b
            wait("tensor", *psum_user[pnm])
            tpe = 0
            for k in range(KF):
                tpe = op("tensor", lambda t, _k=k, _p=psx: t.matmul(
                    _p[:], wouts[:, _k * 128:(_k + 1) * 128], ygb[:, _k * W:(_k + 1) * W],
                    start=(_k == 0), stop=(_k == KF - 1)), "pe")
            wait("vector", "pe", tpe)
            wait("vector", "ss", ss_prev[0])
            tse = op("vector", lambda v, _p=psx: v.tensor_copy(sev[:], _p[:]), "dv")
            psum_user[pnm] = ("dv", tse)
            wait("sync", "dv", tse)
            ss_prev[0] = dma(
                "sync", s_in[c].rearrange("p (k c) -> p k c", k=KD)[:, m:m + 1, :],
                sev[:].unsqueeze(1), "ss")

    emit_ffn(len(ffn_units))

    wait("gpsimd", "ss")
    wait("gpsimd", "zw")
    tcc = op("gpsimd", lambda g: g.collective_compute(
        "AllReduce", ALU.add,
        replica_groups=[[0, 1], [2, 3], [4, 5], [6, 7]],
        ins=[s_in.flatten()], outs=[s_out_d.flatten()]), "cc")

    wait("sync", "cc", tcc)
    for fc in range(4):
        wait("sync", "dv")
        t0 = dma("sync", fs0, s_out_d[fc], "fin")
        t1 = dma("sync", fs1, s_out_d[4 + fc], "fin")
        t2 = dma("sync", fgd[:, 0:KD * TC], gblk[2 * fc], "fin")
        t3 = dma("sync", fgd[:, KD * TC:2 * KD * TC], gblk[2 * fc + 1], "fin")
        wait("vector", "fin", t3)
        op("vector", lambda v: v.tensor_scalar(fs0, fs0, hsel[:, 0:1], None, ALU.mult))
        op("vector", lambda v: v.scalar_tensor_tensor(
            fs0, fs1, hsel[:, 1:2], fs0, ALU.mult, ALU.add), "dv")
        vfill()
        tsg = 0
        for k in range(KD):
            for sub in range(2):
                tsg = op("vector", lambda v, _k=k, _s=sub: v.tensor_tensor(
                    fsg[:, _k * W + _s * TC:_k * W + (_s + 1) * TC],
                    fs0[:, _k * W + _s * TC:_k * W + (_s + 1) * TC],
                    fgd[:, (_s * KD + _k) * TC:(_s * KD + _k + 1) * TC], ALU.mult), "dv")
        for tt in range(4):
            row = fc * 4 + tt
            tlx = load_x(x_mine[row * 128:(row + 1) * 128, :])
            wait("tensor", "dv", tsg)
            tos = 0
            for k in range(KD):
                pnm = "ps_a" if k % 2 == 0 else "ps_b"
                psx = ps_a if k % 2 == 0 else ps_b
                wait("tensor", *psum_user[pnm])
                tpe = op("tensor", lambda t, _k=k, _tt=tt, _p=psx: t.transpose(
                    _p.bitcast(BF16)[:, 0:128], fsg[:, _k * W + _tt * 128:_k * W + _tt * 128 + 128],
                    ident[:]), "pe")
                wait("vector", "pe", tpe)
                if k == 0:
                    wait("vector", "xl", tlx)
                tos = op("vector", lambda v, _k=k, _p=psx: v.tensor_tensor(
                    fot[:, _k * 128:(_k + 1) * 128], xt[:, _k * 128:(_k + 1) * 128],
                    _p.bitcast(BF16)[:, 0:128], ALU.add), "dv")
                psum_user[pnm] = ("dv", tos)
            xt_last["dv"] = tos
            wait("sync", "dv", tos)
            dma("sync", out_half[row * 128:(row + 1) * 128, :], fot[:], "out")
    wait("sync", "out")

    with nc.Block() as block:
        @block.sync
        def _(e):
            for f in prog["sync"]:
                f(e)

        @block.tensor
        def _(e):
            for f in prog["tensor"]:
                f(e)

        @block.vector
        def _(e):
            for f in prog["vector"]:
                f(e)

        @block.scalar
        def _(e):
            for f in prog["scalar"]:
                f(e)

        @block.gpsimd
        def _(e):
            for f in prog["gpsimd"]:
                f(e)

    for g in reversed(guards):
        g.__exit__(None, None, None)
    return nc


_NC_CACHE = [None]


def kernel(L0_attn_x, L0_attn_W_in, L0_attn_conv_w, L0_attn_conv_b, L0_attn_W_x,
           L0_attn_A_log, L0_attn_dt_bias, L0_attn_D, L0_attn_W_out,
           L0_ffn_norm_w, L0_ffn_w_macro, L0_ffn_w_micro, L0_ffn_w1, L0_ffn_w2):
    import ml_dtypes
    x = np.ascontiguousarray(L0_attn_x, np.float32)
    W_in = np.ascontiguousarray(L0_attn_W_in, np.float32)
    conv_w = np.ascontiguousarray(L0_attn_conv_w, np.float32)
    conv_b = np.ascontiguousarray(L0_attn_conv_b, np.float32)
    W_x = np.ascontiguousarray(L0_attn_W_x, np.float32)
    A_log = np.ascontiguousarray(L0_attn_A_log, np.float32)
    dt_bias = np.ascontiguousarray(L0_attn_dt_bias, np.float32)
    Dsk = np.ascontiguousarray(L0_attn_D, np.float32)
    W_out = np.ascontiguousarray(L0_attn_W_out, np.float32)
    norm_w = np.ascontiguousarray(L0_ffn_norm_w, np.float32)
    w_macro = np.ascontiguousarray(L0_ffn_w_macro, np.float32)
    w_micro = np.ascontiguousarray(L0_ffn_w_micro, np.float32)
    w1 = np.ascontiguousarray(L0_ffn_w1, np.float32)
    w2 = np.ascontiguousarray(L0_ffn_w2, np.float32)

    w1T = np.ascontiguousarray(w1.transpose(0, 2, 1))
    w2T = np.ascontiguousarray(w2.transpose(0, 2, 1))
    wmr_cols = np.concatenate([w_macro.T, w_micro.reshape(8, D).T], axis=1)
    wmr_arr = np.ascontiguousarray(
        wmr_cols.reshape(KD, 128, 12).transpose(1, 0, 2).reshape(128, KD * 12))
    dtb_arr = np.ascontiguousarray(dt_bias.reshape(NDT, 128).T)
    cvb_arr = np.ascontiguousarray(conv_b.reshape(NDT, 128).T)
    DD_arr = np.ascontiguousarray(Dsk.reshape(NDT, 128).T)
    nw_arr = np.ascontiguousarray(norm_w.reshape(KD, 128).T)
    cvw_arr = np.ascontiguousarray(
        conv_w.reshape(NDT, 128, 4).transpose(1, 0, 2).reshape(128, NDT * 4))
    ident_np = np.eye(128, dtype=ml_dtypes.bfloat16)
    onesc_np = np.ones((128, 1), dtype=ml_dtypes.bfloat16)
    cstp_np = np.zeros((128, 12), np.float32)
    for _ci, _cv in enumerate(CONST_VALS):
        cstp_np[:, _ci] = _cv

    if _NC_CACHE[0] is None:
        _NC_CACHE[0] = build_program()
    nc = _NC_CACHE[0]

    in_maps = []
    for core in range(8):
        b, j = core // 2, core % 2
        n_lo = j * NNH
        Wx_core = np.ascontiguousarray(np.concatenate(
            [W_x[:, n_lo:n_lo + NNH], W_x[:, 64 + n_lo:64 + n_lo + NNH],
             W_x[:, 128:129]], axis=1))
        A_my = A_log[:, n_lo:n_lo + NNH]
        A_core = np.ascontiguousarray(
            A_my.reshape(NDT, 128, NNH).transpose(1, 0, 2).reshape(128, NDT * NNH))
        hsel_np = np.zeros((128, 2), np.float32)
        hsel_np[:, j] = 1.0
        in_maps.append(dict(
            x_full=x[b],
            x_mine=np.ascontiguousarray(x[b, j * TOK:(j + 1) * TOK]),
            W_in_d=W_in, Wx_my=Wx_core, A_arr=A_core,
            dtb_a=dtb_arr, cvw_a=cvw_arr, cvb_a=cvb_arr, DD_a=DD_arr,
            W_out_d=W_out, nw_a=nw_arr, wmr_a=wmr_arr,
            w1T_d=w1T, w2T_d=w2T,
            ident_d=ident_np, onesc_d=onesc_np, hsel_d=hsel_np,
            cstp_d=cstp_np,
        ))

    res = run_bass_kernel_spmd(nc, in_maps, core_ids=list(range(8)))
    out = np.empty((B, T, D), np.float32)
    for core in range(8):
        b, j = core // 2, core % 2
        out[b, j * TOK:(j + 1) * TOK] = res.results[core]["out_half"]
    return out
